# version 04 (final)
# baseline (speedup 1.0000x reference)
import math
import os
import numpy as np
import ml_dtypes
import concourse.bass as bass
import concourse.mybir as mybir
from concourse.bass_utils import run_bass_kernel_spmd
from contextlib import ExitStack

F32 = mybir.dt.float32
BF16 = mybir.dt.bfloat16
I32 = mybir.dt.int32
AF = mybir.ActivationFunctionType
ALU = mybir.AluOpType
BF = ml_dtypes.bfloat16

S = 8192
D = 2048
NH = 4
VD = 128
NT = 16
OWN = 2048
SCALE = 1.0 / math.sqrt(192.0)
EXP_SHIFT = -30.0
ENGS = ("sync", "scalar", "vector", "tensor", "gpsimd")
MK_PHASES = int(os.environ.get("MK_PHASES", "4"))
MK_T = int(os.environ.get("MK_T", "16"))
MK_SEC = int(os.environ.get("MK_SEC", "9"))


NLANES = 6


class Sched:
    def __init__(self):
        self.C = {}
        self.ops = None
        self.lastw = None
        self.didx = {"sQ": 0, "sG": 0}
        self.last_mark = None

    def new_phase(self):
        self.ops = {e: [] for e in ENGS}
        self.lastw = {e: {} for e in ENGS}
        return self.ops

    def val(self, sem):
        return self.C.get(sem, 0)

    def lanes(self, q):
        return [f"{q}{i}" for i in range(NLANES)]

    def allq(self, q):
        return [(ln, self.C.get(ln, 0)) for ln in self.lanes(q)
                if self.C.get(ln, 0) > 0]

    OWN = {"vector": "sV", "scalar": "sS"}

    def emit(self, e, mk, waits=(), inc=None, inc_by=1):
        wl0 = []
        own = self.OWN.get(e)
        if own is not None:
            v = self.C.get(own, 0)
            if v > 0:
                wl0.append((own, v))
        for w in waits:
            if w is None:
                continue
            sname, v = w
            if sname in ("sQ", "sG") and v == "ALL":
                wl0.extend(self.allq(sname))
            else:
                wl0.append((sname, v))
        if inc in ("sQ", "sG"):
            lane = self.lanes(inc)[self.didx[inc] % NLANES]
            self.didx[inc] += 1
            cur = self.C.get(lane, 0)
            if cur > 0:
                wl0.append((lane, cur))
            inc = lane
            inc_by = 16
        wl = []
        for sname, v in wl0:
            if v > 0 and self.lastw[e].get(sname, -1) < v:
                wl.append((sname, v))
                self.lastw[e][sname] = v
        if inc is None and own is not None:
            inc = own
        if inc is not None:
            self.C[inc] = self.C.get(inc, 0) + inc_by
            if inc[:2] in ("sQ", "sG"):
                self.last_mark = (inc, self.C[inc])
        self.ops[e].append((mk, tuple(wl), inc, inc_by))


def runner(oplist, sems):
    def f(eng):
        for mk, waits, isem, iby in oplist:
            for s, v in waits:
                eng.wait_ge(sems[s], v)
            ins = mk(eng)
            if isem is not None:
                ins.then_inc(sems[isem], iby)
    return f


def build_nc():
    nc = bass.Bass("TRN2", target_bir_lowering=False)

    xT = nc.dram_tensor("xT", [D, S], F32, kind="ExternalInput").ap()
    xTo = nc.dram_tensor("xTo", [D, OWN], F32, kind="ExternalInput").ap()
    wq = nc.dram_tensor("wq", [D, 1024], F32, kind="ExternalInput").ap()
    wkva = nc.dram_tensor("wkva", [D, 768], F32, kind="ExternalInput").ap()
    wkvb = nc.dram_tensor("wkvb", [512, 1024], F32, kind="ExternalInput").ap()
    kvgain = nc.dram_tensor("kvgain", [128, 4], F32, kind="ExternalInput").ap()
    wo = nc.dram_tensor("wo", [D, D], F32, kind="ExternalInput").ap()
    wgT = nc.dram_tensor("wgT", [D, 8192], I32, kind="ExternalInput").ap()
    wuT = nc.dram_tensor("wuT", [D, 8192], I32, kind="ExternalInput").ap()
    wdT = nc.dram_tensor("wdT", [8192, D], I32, kind="ExternalInput").ap()
    gsc = nc.dram_tensor("gsc", [128, 64], F32, kind="ExternalInput").ap()
    usc = nc.dram_tensor("usc", [128, 64], F32, kind="ExternalInput").ap()
    dsc = nc.dram_tensor("dsc", [128, 16], F32, kind="ExternalInput").ap()
    ropeC = nc.dram_tensor("ropeC", [128, S], F32, kind="ExternalInput").ap()
    ropeS = nc.dram_tensor("ropeS", [128, S], F32, kind="ExternalInput").ap()
    tri = nc.dram_tensor("tri", [128, 128], BF16, kind="ExternalInput").ap()
    ident = nc.dram_tensor("ident", [128, 128], BF16, kind="ExternalInput").ap()
    gidx = nc.dram_tensor("gidx", [128, 64], mybir.dt.uint32, kind="ExternalInput").ap()
    sidx = nc.dram_tensor("sidx", [128, 4], mybir.dt.uint32, kind="ExternalInput").ap()
    outT = nc.dram_tensor("outT", [D, OWN], F32, kind="ExternalOutput").ap()

    qn_sc = nc.dram_tensor("qn_sc", [NH, 128, S], BF16, kind="Internal").ap()
    qr_sc = nc.dram_tensor("qr_sc", [NH, 64, S], BF16, kind="Internal").ap()
    kn_sc = nc.dram_tensor("kn_sc", [NH, 128, S], BF16, kind="Internal").ap()
    kr_sc = nc.dram_tensor("kr_sc", [64, S], BF16, kind="Internal").ap()
    v_sc = nc.dram_tensor("v_sc", [NH, S // 128, 128, VD], BF16, kind="Internal").ap()
    aT_in = nc.dram_tensor("aT_in", [D, S], BF16, kind="Internal").ap()
    aT_all = nc.dram_tensor("aT_all", [D, S], BF16, kind="Internal").ap()
    x2_sc = nc.dram_tensor("x2_sc", [D, OWN], F32, kind="Internal").ap()
    xn2_sc = nc.dram_tensor("xn2_sc", [D, OWN], BF16, kind="Internal").ap()

    SC = Sched()
    st = ExitStack()

    ones_c = st.enter_context(nc.sbuf_tensor("ones_c", [128, 1], F32))
    ones_r = st.enter_context(nc.sbuf_tensor("ones_r", [1, 128], F32))
    ones_b = st.enter_context(nc.sbuf_tensor("ones_b", [128, 1], BF16))
    m128r = st.enter_context(nc.sbuf_tensor("m128r", [1, 128], F32))
    csc = st.enter_context(nc.sbuf_tensor("cs", [128, 1], F32))
    cshift = st.enter_context(nc.sbuf_tensor("csh", [128, 1], F32))
    ceps = st.enter_context(nc.sbuf_tensor("ceps", [1, 1], F32))
    cepsr = st.enter_context(nc.sbuf_tensor("cepsr", [1, 1], F32))
    chalf = st.enter_context(nc.sbuf_tensor("chalf", [1, 1], F32))
    tri_s = st.enter_context(nc.sbuf_tensor("tri_s", [128, 128], BF16))
    id_s = st.enter_context(nc.sbuf_tensor("id_s", [128, 128], BF16))
    gsc_s = st.enter_context(nc.sbuf_tensor("gsc_s", [128, 64], F32))
    usc_s = st.enter_context(nc.sbuf_tensor("usc_s", [128, 64], F32))
    dsc_s = st.enter_context(nc.sbuf_tensor("dsc_s", [128, 16], F32))
    dsc2_s = st.enter_context(nc.sbuf_tensor("dsc2_s", [128, 16], F32))
    kvg_s = st.enter_context(nc.sbuf_tensor("kvg_s", [128, 4], F32))

    sems = {}
    semnames = ["sP", "sS", "sV", "sB", "sC"]
    semnames += [f"sQ{i}" for i in range(6)] + [f"sG{i}" for i in range(6)]
    for s in semnames:
        sems[s] = st.enter_context(nc.semaphore(s))

    def barrier_out():
        for e in ENGS:
            wv = ()
            if e == "gpsimd":
                wv = tuple(SC.allq("sG"))
            elif e == "sync":
                wv = tuple(SC.allq("sQ"))
            SC.emit(e, lambda g: g.nop(), waits=wv, inc="sB")

    def barrier_in(ph):
        for e in ENGS:
            SC.emit(e, lambda g: g.nop(), waits=(("sB", 5 * ph),))

    def ln_rows(rowt, i_mu, i_sq, i_var, i_mr, dep_sem, dep_val, eps_ap, invn):
        SC.emit("vector", (lambda: lambda g: g.tensor_mul(
            rowt[0:1, i_var, :], rowt[0:1, i_mu, :], rowt[0:1, i_mu, :]))(),
            waits=((dep_sem, dep_val),))
        SC.emit("vector", (lambda: lambda g: g.tensor_sub(
            rowt[0:1, i_var, :], rowt[0:1, i_sq, :], rowt[0:1, i_var, :]))(),
            inc="sV")
        vv = SC.val("sV")
        SC.emit("scalar", (lambda: lambda g: g.activation(
            rowt[0:1, i_var, :], rowt[0:1, i_var, :], AF.Ln, bias=eps_ap))(),
            waits=(("sV", vv),))
        SC.emit("scalar", (lambda: lambda g: g.activation(
            rowt[0:1, i_var, :], rowt[0:1, i_var, :], AF.Exp,
            scale=chalf[0:1, 0:1]))(), inc="sS")
        ss = SC.val("sS")
        SC.emit("vector", (lambda: lambda g: g.tensor_mul(
            rowt[0:1, i_mr, :], rowt[0:1, i_mu, :], rowt[0:1, i_var, :]))(),
            waits=(("sS", ss),), inc="sV")
        return SC.val("sV")

    with ExitStack() as p1:
        SC.new_phase()
        e = p1.enter_context
        wq_bf = e(nc.sbuf_tensor("wq_bf", [128, 16, 1024], BF16))
        wkva_bf = e(nc.sbuf_tensor("wkva_bf", [128, 16, 768], BF16))
        wkvb_bf = e(nc.sbuf_tensor("wkvb_bf", [128, 4, 1024], BF16))
        wstage = e(nc.sbuf_tensor("wstage", [128, 2, 1024], F32))
        xt = e(nc.sbuf_tensor("xt", [128, 16, 512], F32))
        xn = e(nc.sbuf_tensor("xn", [128, 16, 512], BF16))
        xsq = e(nc.sbuf_tensor("xsq", [128, 2, 512], BF16))
        xtmp = e(nc.sbuf_tensor("xtmp", [128, 512], F32))
        xtmp2 = e(nc.sbuf_tensor("xtmp2", [128, 512], F32))
        csq = e(nc.sbuf_tensor("csq", [128, 4, 512], BF16))
        ckvf = e(nc.sbuf_tensor("ckvf", [128, 4, 512], F32))
        ckvn = e(nc.sbuf_tensor("ckvn", [128, 4, 512], BF16))
        rows = e(nc.sbuf_tensor("rows", [1, 8, 512], F32))
        krf = e(nc.sbuf_tensor("krf", [64, 512], F32))
        krf2 = e(nc.sbuf_tensor("krf2", [64, 512], F32))
        krbf = e(nc.sbuf_tensor("krbf", [64, 512], BF16))
        rc = e(nc.sbuf_tensor("rc", [128, 512], F32))
        rs = e(nc.sbuf_tensor("rs", [128, 512], F32))
        qaf = e(nc.sbuf_tensor("qaf", [128, 2, 512], F32))
        qbf = e(nc.sbuf_tensor("qbf", [128, 2, 512], F32))
        qst = e(nc.sbuf_tensor("qst", [128, 2, 512], BF16))
        qrbf = e(nc.sbuf_tensor("qrbf", [128, 2, 512], BF16))
        knst = e(nc.sbuf_tensor("knst", [128, 2, 512], BF16))
        vst = e(nc.sbuf_tensor("vst", [128, 4, 512], BF16))
        zbuf = e(nc.sbuf_tensor("zbuf", [128, 2048], BF16))
        pb = [e(nc.psum_tensor(f"pb{i}", [128, 512], F32)) for i in range(8)]

        for mk in (lambda g: g.memset(ones_c[:], 1.0),
                   lambda g: g.memset(ones_r[:], 1.0),
                   lambda g: g.memset(ones_b[:], 1.0),
                   lambda g: g.memset(m128r[:], -128.0),
                   lambda g: g.memset(csc[:], SCALE),
                   lambda g: g.memset(cshift[:], EXP_SHIFT),
                   lambda g: g.memset(ceps[:], 1e-5),
                   lambda g: g.memset(cepsr[:], 2.0 ** -23),
                   lambda g: g.memset(chalf[:], -0.5)):
            SC.emit("vector", mk)
        SC.emit("vector", lambda g: g.nop(), inc="sV")
        cdone = SC.val("sV")

        for name, ap_, dst in (("tri", tri, tri_s), ("id", ident, id_s),
                               ("gsc", gsc, gsc_s), ("usc", usc, usc_s),
                               ("dsc", dsc, dsc_s), ("kvg", kvgain, kvg_s)):
            SC.emit("sync", (lambda ap_=ap_, dst=dst: lambda g: g.dma_start(
                dst[:], ap_))(), inc="sQ", inc_by=16)
        SC.emit("vector", lambda g: g.tensor_mul(dsc2_s[:], dsc_s[:], dsc_s[:]),
                waits=(("sQ", "ALL"),), inc="sV")
        SC.emit("vector", lambda g: g.memset(zbuf[:], 0.0), inc="sV")
        zv = SC.val("sV")
        for zr in range(16):
            for zc in range(4):
                SC.emit("gpsimd", (lambda zr=zr, zc=zc: lambda g: g.dma_start(
                    aT_in[zr * 128:(zr + 1) * 128, zc * 2048:(zc + 1) * 2048],
                    zbuf[:]))(), waits=(("sV", zv),), inc="sG", inc_by=16)

        cast_marks = []

        def stream_w(src_ap, n_tiles, width, dst_fn, via_vec_gain=None):
            for i in range(n_tiles):
                slot = len(cast_marks) % 2
                wv = []
                if len(cast_marks) >= 2:
                    cm = cast_marks[-2]
                    wv.append(cm)
                SC.emit("sync", (lambda i=i, slot=slot, src_ap=src_ap, width=width:
                                 lambda g: g.dma_start(
                                     wstage[:, slot, 0:width],
                                     src_ap[i * 128:(i + 1) * 128, :]))(),
                        waits=wv, inc="sQ", inc_by=16)
                lq = SC.last_mark
                if via_vec_gain is None:
                    SC.emit("scalar", (lambda i=i, slot=slot, width=width, dst_fn=dst_fn:
                                       lambda g: g.activation(
                                           dst_fn(i), wstage[:, slot, 0:width], AF.Copy))(),
                            waits=(lq,), inc="sS")
                    cast_marks.append(("sS", SC.val("sS")))
                else:
                    SC.emit("vector", (lambda i=i, slot=slot, width=width, dst_fn=dst_fn, gv=via_vec_gain:
                                       lambda g: g.tensor_scalar(
                                           dst_fn(i), wstage[:, slot, 0:width],
                                           gv(i), None, ALU.mult))(),
                            waits=(lq, ("sV", cdone)), inc="sV")
                    cast_marks.append(("sV", SC.val("sV")))

        stream_w(wq, 16, 1024, lambda i: wq_bf[:, i, :])
        stream_w(wkva, 16, 768, lambda i: wkva_bf[:, i, :])
        stream_w(wkvb, 4, 1024, lambda i: wkvb_bf[:, i, :],
                 via_vec_gain=lambda i: kvg_s[:, i:i + 1])
        w_s_ready = SC.val("sS")
        w_v_ready = SC.val("sV")

        blk_xt_free = [0] * 16
        xn_marks_prev = 0
        prev_q_evac = 0
        rc_ld = None
        prev_rope_rd = 0
        prev_kv_evac = 0
        vst_marks = []
        knst_mark = [None, None]
        qst_mark = [None, None]
        qr_mark = [[], []]
        kr_mark = None
        out_mark = [None, None]
        prev_qr_rd = 0
        prev_ckvn_rd = 0

        for t in range(MK_T):
            tok0 = 512 * t
            ld_marks = []
            for d in range(16):
                wv = [("sV", blk_xt_free[d])] if t > 0 else ()
                SC.emit("sync", (lambda d=d, tok0=tok0: lambda g: g.dma_start(
                    xt[:, d, :], xT[d * 128:(d + 1) * 128, tok0:tok0 + 512]))(),
                    waits=wv, inc="sQ", inc_by=16)
                ld_marks.append(SC.last_mark)
            sqp_marks = []
            for d in range(16):
                wv = [ld_marks[d], ("sV", cdone)]
                if d >= 2:
                    wv.append(("sP", sqp_marks[d - 2]))
                if t > 0 and d == 0:
                    wv.append(("sV", xn_marks_prev))
                SC.emit("scalar", (lambda d=d: lambda g: g.activation(
                    xn[:, d, :], xt[:, d, :], AF.Copy))(), waits=wv)
                SC.emit("scalar", (lambda d=d: lambda g: g.activation(
                    xsq[:, d % 2, :], xt[:, d, :], AF.Square))(), inc="sS")
                sqs = SC.val("sS")
                wv2 = [("sS", sqs)]
                if d == 0:
                    wv2.append(("sS", prev_q_evac if t > 0 else w_s_ready))
                SC.emit("tensor", (lambda d=d: lambda g: g.matmul(
                    pb[0][0:1, :], ones_b[:], xn[:, d, :],
                    start=(d == 0), stop=(d == 15)))(), waits=wv2)
                SC.emit("tensor", (lambda d=d: lambda g: g.matmul(
                    pb[1][0:1, :], ones_b[:], xsq[:, d % 2, :],
                    start=(d == 0), stop=(d == 15)))(), inc="sP")
                sqp_marks.append(SC.val("sP"))
            stats_done = SC.val("sP")
            SC.emit("vector", lambda g: g.tensor_scalar_mul(
                rows[0:1, 0, :], pb[0][0:1, :], 1.0 / 2048.0),
                waits=(("sP", stats_done),))
            SC.emit("vector", lambda g: g.tensor_scalar_mul(
                rows[0:1, 1, :], pb[1][0:1, :], 1.0 / 2048.0), inc="sV")
            mr_done = ln_rows(rows, 0, 1, 2, 3, "sV", SC.val("sV"),
                              ceps[0:1, 0:1], None)
            SC.emit("tensor", lambda g: g.matmul(
                pb[0][:, :], ones_r[:], rows[0:1, 2, :], start=True, stop=True),
                waits=(("sV", mr_done),))
            SC.emit("tensor", lambda g: g.matmul(
                pb[1][:, :], ones_r[:], rows[0:1, 3, :], start=True, stop=True),
                inc="sP")
            bc_done = SC.val("sP")
            xn_marks = []
            for d in range(16):
                SC.emit("vector", (lambda d=d: lambda g: g.tensor_tensor(
                    xtmp[:], xt[:, d, :], pb[0][:, :], ALU.mult))(),
                    waits=(("sP", bc_done),))
                SC.emit("vector", (lambda d=d: lambda g: g.tensor_tensor(
                    xn[:, d, :], xtmp[:], pb[1][:, :], ALU.subtract))(), inc="sV")
                blk_xt_free[d] = SC.val("sV")
                xn_marks.append(SC.val("sV"))
            xn_marks_prev = xn_marks[15]
            if MK_SEC < 2:
                continue
            for m in range(6):
                for d in range(16):
                    wv = [("sV", xn_marks[d])]
                    if d == 0:
                        wv.append(("sV", w_v_ready))
                        if t > 0:
                            wv.append(("sS", prev_kv_evac if m < 4 else prev_rope_rd))
                            wv.append(("sV", prev_ckvn_rd))
                    SC.emit("tensor", (lambda m=m, d=d: lambda g: g.matmul(
                        pb[2 + m][:, :], wkva_bf[:, d, m * 128:(m + 1) * 128],
                        xn[:, d, :], start=(d == 0), stop=(d == 15)))(),
                        waits=wv,
                        inc="sP" if (m == 5 and d == 15) else None)
            kva_done = SC.val("sP")
            if MK_SEC < 3:
                continue
            SUB = int(os.environ.get("MK_RMS", "9"))
            for j in range(4):
                SC.emit("scalar", (lambda j=j: lambda g: g.activation(
                    csq[:, j, :], pb[2 + j][:, :], AF.Square))(),
                    waits=(("sP", kva_done),) if j == 0 else ())
                SC.emit("scalar", (lambda j=j: lambda g: g.activation(
                    ckvf[:, j, :], pb[2 + j][:, :], AF.Copy))(), inc="sS")
            csq_done = SC.val("sS")
            if SUB < 2:
                continue
            for j in range(4):
                wv = [("sS", csq_done), ("sV", blk_xt_free[15])]
                SC.emit("tensor", (lambda j=j: lambda g: g.matmul(
                    pb[0][0:1, :], ones_b[:], csq[:, j, :],
                    start=(j == 0), stop=(j == 3)))(), waits=wv if j == 0 else (),
                    inc="sP" if j == 3 else None)
            if SUB < 3:
                continue
            SC.emit("vector", lambda g: g.tensor_scalar_mul(
                rows[0:1, 4, :], pb[0][0:1, :], 1.0 / 512.0),
                waits=(("sP", SC.val("sP")),), inc="sV")
            SC.emit("scalar", lambda g: g.activation(
                rows[0:1, 4, :], rows[0:1, 4, :], AF.Ln, bias=cepsr[:]),
                waits=(("sV", SC.val("sV")),))
            SC.emit("scalar", lambda g: g.activation(
                rows[0:1, 4, :], rows[0:1, 4, :], AF.Exp, scale=chalf[0:1, 0:1]),
                inc="sS")
            if SUB < 4:
                continue
            SC.emit("tensor", lambda g: g.matmul(
                pb[0][:, :], ones_r[:], rows[0:1, 4, :], start=True, stop=True),
                waits=(("sS", SC.val("sS")),), inc="sP")
            rrmsb = SC.val("sP")
            if SUB < 5:
                continue
            ckvn_marks = []
            for j in range(4):
                SC.emit("vector", (lambda j=j: lambda g: g.tensor_tensor(
                    ckvn[:, j, :], ckvf[:, j, :], pb[0][:, :], ALU.mult))(),
                    waits=(("sP", rrmsb),) if j == 0 else (), inc="sV")
                ckvn_marks.append(SC.val("sV"))
            if MK_SEC < 4:
                continue
            SC.emit("sync", (lambda tok0=tok0: lambda g: g.dma_start(
                rc[:], ropeC[:, tok0:tok0 + 512]))(),
                waits=(("sV", prev_qr_rd),), inc="sQ", inc_by=16)
            rc_ld = SC.last_mark
            SC.emit("sync", (lambda tok0=tok0: lambda g: g.dma_start(
                rs[:], ropeS[:, tok0:tok0 + 512]))(), inc="sQ", inc_by=16)
            rope_ld = SC.last_mark
            SC.emit("scalar", lambda g: g.activation(
                krf[0:64, :], pb[6][0:64, :], AF.Copy),
                waits=(("sP", kva_done),), inc="sS")
            SC.emit("scalar", lambda g: g.activation(
                krf2[0:64, :], pb[7][0:64, :], AF.Copy), inc="sS")
            prev_rope_rd = SC.val("sS")
            SC.emit("vector", lambda g: g.tensor_tensor(
                xtmp[0:64, :], krf[0:64, :], rc[0:64, :], ALU.mult),
                waits=(("sS", prev_rope_rd), rc_ld, rope_ld))
            SC.emit("vector", lambda g: g.tensor_tensor(
                xtmp2[0:64, :], krf2[0:64, :], rs[0:64, :], ALU.mult))
            SC.emit("vector", lambda g: g.tensor_add(
                krbf[:], xtmp[0:64, :], xtmp2[0:64, :]),
                waits=(kr_mark,) if kr_mark else (), inc="sV")
            kr_done = SC.val("sV")
            SC.emit("gpsimd", (lambda tok0=tok0: lambda g: g.dma_start(
                kr_sc[:, tok0:tok0 + 512], krbf[:]))(),
                waits=(("sV", kr_done),), inc="sG", inc_by=16)
            kr_mark = SC.last_mark
            if MK_SEC < 5:
                continue
            for h in range(NH):
                bnk = 2 + (h % 2)
                wv = [("sV", ckvn_marks[3]), ("sS", csq_done)]
                if h >= 2:
                    wv.append(("sS", SC.val("sS")))
                for j in range(4):
                    SC.emit("tensor", (lambda h=h, j=j, bnk=bnk: lambda g: g.matmul(
                        pb[bnk][:, :], wkvb_bf[:, j, h * 128:(h + 1) * 128],
                        ckvn[:, j, :], start=(j == 0), stop=(j == 3)))(),
                        waits=wv if j == 0 else (),
                        inc="sP" if j == 3 else None)
                kh_done = SC.val("sP")
                SC.emit("scalar", (lambda h=h, bnk=bnk: lambda g: g.activation(
                    knst[:, h % 2, :], pb[bnk][:, :], AF.Copy))(),
                    waits=[("sP", kh_done)] + ([knst_mark[h % 2]] if knst_mark[h % 2] else []), inc="sS")
                SC.emit("gpsimd", (lambda h=h, tok0=tok0: lambda g: g.dma_start(
                    kn_sc[h, :, tok0:tok0 + 512], knst[:, h % 2, :]))(),
                    waits=(("sS", SC.val("sS")),), inc="sG", inc_by=16)
                knst_mark[h % 2] = SC.last_mark
            if MK_SEC < 6:
                continue
            for ks in range(4):
                bnk = 4 + (ks % 2)
                wv = [("sV", ckvn_marks[3]), ("sS", csq_done)]
                if ks >= 2:
                    wv.append(("sS", SC.val("sS")))
                if ks == 0:
                    wv.extend(vst_marks)
                for j in range(4):
                    SC.emit("tensor", (lambda ks=ks, j=j, bnk=bnk: lambda g: g.matmul(
                        pb[bnk][:, :], ckvn[:, j, ks * 128:(ks + 1) * 128],
                        wkvb_bf[:, j, 512:1024], start=(j == 0), stop=(j == 3)))(),
                        waits=wv if j == 0 else (),
                        inc="sP" if j == 3 else None)
                vk_done = SC.val("sP")
                SC.emit("scalar", (lambda ks=ks, bnk=bnk: lambda g: g.activation(
                    vst[:, ks, :], pb[bnk][:, :], AF.Copy))(),
                    waits=[("sP", vk_done)] + vst_marks, inc="sS")
            vst_done = SC.val("sS")
            vst_marks = []
            for h in range(NH):
                SC.emit("gpsimd", (lambda h=h, t=t: lambda g: g.dma_start(
                    v_sc[h, t * 4:(t + 1) * 4, :, :].rearrange("g p d -> p g d"),
                    vst[:, :, h * 128:(h + 1) * 128]))(),
                    waits=(("sS", vst_done),), inc="sG", inc_by=16)
                vst_marks.append(SC.last_mark)
            prev_kv_evac = vst_done
            if MK_SEC < 7:
                continue
            for m in range(8):
                bnk = m % 2
                wv = [("sV", xn_marks[15])]
                if m >= 2:
                    wv.append(("sS", SC.val("sS")))
                else:
                    wv.append(("sV", ckvn_marks[3]))
                    if t == 0:
                        wv.append(("sS", w_s_ready))
                for d in range(16):
                    SC.emit("tensor", (lambda m=m, d=d, bnk=bnk: lambda g: g.matmul(
                        pb[bnk][:, :], wq_bf[:, d, m * 128:(m + 1) * 128],
                        xn[:, d, :], start=(d == 0), stop=(d == 15)))(),
                        waits=wv if d == 0 else (),
                        inc="sP" if d == 15 else None)
                qm_done = SC.val("sP")
                if m < 4:
                    SC.emit("scalar", (lambda m=m, bnk=bnk: lambda g: g.activation(
                        qst[:, m % 2, :], pb[bnk][:, :], AF.Copy))(),
                        waits=[("sP", qm_done)] + ([qst_mark[m % 2]] if qst_mark[m % 2] else []), inc="sS")
                    SC.emit("gpsimd", (lambda m=m, tok0=tok0: lambda g: g.dma_start(
                        qn_sc[m, :, tok0:tok0 + 512], qst[:, m % 2, :]))(),
                        waits=(("sS", SC.val("sS")),), inc="sG", inc_by=16)
                    qst_mark[m % 2] = SC.last_mark
                elif m < 6:
                    SC.emit("scalar", (lambda m=m, bnk=bnk: lambda g: g.activation(
                        qaf[:, m - 4, :], pb[bnk][:, :], AF.Copy))(),
                        waits=(("sP", qm_done), ("sV", prev_qr_rd)), inc="sS")
                else:
                    SC.emit("scalar", (lambda m=m, bnk=bnk: lambda g: g.activation(
                        qbf[:, m - 6, :], pb[bnk][:, :], AF.Copy))(),
                        waits=(("sP", qm_done), ("sV", prev_qr_rd)), inc="sS")
            prev_q_evac = SC.val("sS")
            qab_done = SC.val("sS")
            for j in range(2):
                SC.emit("vector", (lambda j=j: lambda g: g.tensor_tensor(
                    xtmp[:], qaf[:, j, :], rc[:], ALU.mult))(),
                    waits=(("sS", qab_done),))
                SC.emit("vector", (lambda j=j: lambda g: g.tensor_tensor(
                    xtmp2[:], qbf[:, j, :], rs[:], ALU.mult))())
                SC.emit("vector", (lambda j=j: lambda g: g.tensor_add(
                    qrbf[:, j, :], xtmp[:], xtmp2[:]))(),
                    waits=tuple(qr_mark[j]), inc="sV")
                qr_done = SC.val("sV")
                SC.emit("gpsimd", (lambda j=j, tok0=tok0: lambda g: g.dma_start(
                    qr_sc[2 * j, :, tok0:tok0 + 512], qrbf[0:64, j, :]))(),
                    waits=(("sV", qr_done),), inc="sG", inc_by=16)
                qr_mark[j] = [SC.last_mark]
                SC.emit("gpsimd", (lambda j=j, tok0=tok0: lambda g: g.dma_start(
                    qr_sc[2 * j + 1, :, tok0:tok0 + 512], qrbf[64:128, j, :]))(),
                    inc="sG", inc_by=16)
                qr_mark[j].append(SC.last_mark)
            prev_qr_rd = SC.val("sV")
            prev_ckvn_rd = SC.val("sV")

        barrier_out()
        with nc.Block("ph1") as blk:
            for en in ENGS:
                getattr(blk, en)(runner(SC.ops[en], sems))

    if MK_PHASES < 2:
        st.close()
        return nc
    with ExitStack() as p2:
        SC.new_phase()
        barrier_in(1)
        e = p2.enter_context
        qn_ch = e(nc.sbuf_tensor("qn_ch", [128, NH, 512], BF16))
        qr_ch = e(nc.sbuf_tensor("qr_ch", [64, NH, 512], BF16))
        kn_t = e(nc.sbuf_tensor("kn_t", [128, 2, NH, 512], BF16))
        kr_t = e(nc.sbuf_tensor("kr_t", [64, 2, 512], BF16))
        vt = e(nc.sbuf_tensor("vt", [128, 2, NH, 4, VD], BF16))
        prb = e(nc.sbuf_tensor("prb", [128, 2, 512], BF16))
        rl = e(nc.sbuf_tensor("rl", [128, 16], F32))
        ao = e(nc.sbuf_tensor("ao", [128, 4, 512], BF16))
        aoT = e(nc.sbuf_tensor("aoT", [128, 4, 512], BF16))
        pO = [e(nc.psum_tensor(f"pO{i}", [128, 512], F32)) for i in range(4)]
        pL = e(nc.psum_tensor("pL", [128, 16], F32))
        pS = [e(nc.psum_tensor(f"pS{i}", [128, 512], F32)) for i in range(2)]
        pT = e(nc.psum_tensor("pT", [128, 128], BF16))
        sidx_s = e(nc.sbuf_tensor("sidx_s", [128, 4], mybir.dt.uint32))
        SC.emit("sync", lambda g: g.dma_start(sidx_s[:], sidx), inc="sQ", inc_by=16)
        six = SC.last_mark

        exp_marks = []
        fin_v = 0
        unit = 0
        for ch in range(16):
            qs = 512 * ch
            wv = [("sP", SC.val("sP"))]
            SC.emit("sync", (lambda qs=qs: lambda g: g.dma_start(
                qn_ch[:], qn_sc[:, :, qs:qs + 512].rearrange("h p t -> p h t")))(),
                waits=wv, inc="sQ", inc_by=16)
            qn_mark = SC.last_mark
            SC.emit("sync", (lambda qs=qs: lambda g: g.dma_start(
                qr_ch[:], qr_sc[:, :, qs:qs + 512].rearrange("h p t -> p h t")))(),
                inc="sQ", inc_by=16)
            q_ld = [qn_mark, SC.last_mark]
            first_O = [True] * 4
            first_L = True
            for kt in range(ch + 1):
                k0 = 512 * kt
                pg = kt % 2
                diag = (kt == ch)
                wv = [("sP", SC.val("sP"))] if kt >= 2 else ()
                SC.emit("sync", (lambda k0=k0, pg=pg: lambda g: g.dma_start(
                    kn_t[:, pg, :, :],
                    kn_sc[:, :, k0:k0 + 512].rearrange("h p t -> p h t")))(),
                    waits=wv, inc="sQ", inc_by=16)
                tile_ld = [SC.last_mark]
                SC.emit("sync", (lambda k0=k0, pg=pg: lambda g: g.dma_start(
                    kr_t[:, pg, :], kr_sc[:, k0:k0 + 512]))(), inc="sQ", inc_by=16)
                tile_ld.append(SC.last_mark)
                for h in range(NH):
                    SC.emit("sync", (lambda h=h, kt=kt, pg=pg: lambda g: g.dma_start(
                        vt[:, pg, h, :, :],
                        v_sc[h, kt * 4:kt * 4 + 4, :, :].rearrange("g p d -> p g d")))(),
                        inc="sQ", inc_by=16)
                    tile_ld.append(SC.last_mark)
                for h in range(NH):
                    for ks in range(4):
                        qc0 = 128 * ks if diag else 0
                        sb = unit % 2
                        unit += 1
                        wv = tile_ld + q_ld
                        if len(exp_marks) >= 2:
                            wv.append(("sS", exp_marks[-2]))
                        SC.emit("tensor",
                                (lambda h=h, ks=ks, pg=pg, sb=sb, qc0=qc0: lambda g: g.matmul(
                                    pS[sb][:, qc0:512],
                                    kn_t[:, pg, h, ks * 128:(ks + 1) * 128],
                                    qn_ch[:, h, qc0:512], start=True, stop=False))(),
                                waits=wv)
                        SC.emit("tensor",
                                (lambda h=h, ks=ks, pg=pg, sb=sb, qc0=qc0: lambda g: g.matmul(
                                    pS[sb][:, qc0:512],
                                    kr_t[:, pg, ks * 128:(ks + 1) * 128],
                                    qr_ch[:, h, qc0:512], start=False, stop=True))(),
                                inc="sP")
                        sc_done = SC.val("sP")
                        SC.emit("scalar",
                                (lambda sb=sb, qc0=qc0: lambda g: g.activation(
                                    prb[:, sb, qc0:512], pS[sb][:, qc0:512], AF.Exp,
                                    bias=cshift[:], scale=csc[:]))(),
                                waits=(("sP", sc_done),), inc="sS")
                        exp_marks.append(SC.val("sS"))
                        ex_done = SC.val("sS")
                        msk = None
                        if diag:
                            SC.emit("vector",
                                    (lambda sb=sb, ks=ks: lambda g: g.tensor_mul(
                                        prb[:, sb, ks * 128:(ks + 1) * 128],
                                        prb[:, sb, ks * 128:(ks + 1) * 128],
                                        tri_s[:]))(),
                                    waits=(("sS", ex_done),), inc="sV")
                            msk = SC.val("sV")
                        for qsub in range(ks if diag else 0, 4):
                            wv = [("sS", ex_done)]
                            if msk is not None and qsub == ks:
                                wv.append(("sV", msk))
                            sO = first_O[qsub]
                            if sO:
                                wv.append(("sV", fin_v))
                                first_O[qsub] = False
                            SC.emit("tensor",
                                    (lambda h=h, sb=sb, qsub=qsub, pg=pg, ks=ks, sO=sO:
                                     lambda g: g.matmul(
                                         pO[qsub][:, h * 128:(h + 1) * 128],
                                         prb[:, sb, qsub * 128:(qsub + 1) * 128],
                                         vt[:, pg, h, ks, :], start=sO, stop=False,
                                         skip_group_check=True))(),
                                    waits=wv)
                            sL = first_L
                            first_L = False
                            SC.emit("tensor",
                                    (lambda h=h, sb=sb, qsub=qsub, sL=sL:
                                     lambda g: g.matmul(
                                         pL[:, (h * 4 + qsub):(h * 4 + qsub) + 1],
                                         prb[:, sb, qsub * 128:(qsub + 1) * 128],
                                         ones_b[:], start=sL, stop=False,
                                         skip_group_check=True))(),
                                    inc="sP")
            av_done = SC.val("sP")
            SC.emit("vector", lambda g: g.reciprocal(rl[:], pL[:, :]),
                    waits=(("sP", av_done),), inc="sV")
            rldone = SC.val("sV")
            for h in range(NH):
                for qsub in range(4):
                    SC.emit("vector",
                            (lambda h=h, qsub=qsub: lambda g: g.tensor_scalar(
                                ao[:, qsub, h * 128:(h + 1) * 128],
                                pO[qsub][:, h * 128:(h + 1) * 128],
                                rl[:, (h * 4 + qsub):(h * 4 + qsub) + 1],
                                None, ALU.mult))(),
                            waits=(("sV", rldone),), inc="sV")
            fin_v = SC.val("sV")
            for hs4 in range(4):
                for qsub in range(4):
                    wv = [("sV", fin_v), ("sS", SC.val("sS"))]
                    SC.emit("tensor",
                            (lambda hs4=hs4, qsub=qsub: lambda g: g.transpose(
                                pT[:], ao[:, qsub, hs4 * 128:(hs4 + 1) * 128],
                                id_s[:]))(),
                            waits=wv, inc="sP")
                    tp_done = SC.val("sP")
                    SC.emit("scalar",
                            (lambda hs4=hs4, qsub=qsub: lambda g: g.activation(
                                aoT[:, hs4, qsub * 128:(qsub + 1) * 128], pT[:],
                                AF.Copy))(),
                            waits=[("sP", tp_done)] + SC.allq("sG"), inc="sS")
            fin_s = SC.val("sS")
            for hs4 in range(4):
                SC.emit("gpsimd",
                        (lambda hs4=hs4, qs=qs: lambda g: g.indirect_dma_start(
                            aT_in,
                            bass.IndirectOffsetOnAxis(sidx_s[:, hs4:hs4 + 1], 0),
                            aoT[:, hs4, :], None, element_offset=qs))(),
                        waits=(("sS", fin_s), six), inc="sG", inc_by=16)
        SC.emit("gpsimd", lambda g: g.collective_compute(
            "AllReduce", ALU.add,
            replica_groups=[[0, 1, 2, 3], [4, 5, 6, 7]],
            ins=[aT_in.opt()], outs=[aT_all.opt()]),
            waits=tuple(SC.allq("sG")), inc="sC", inc_by=1)
        barrier_out()
        with nc.Block("ph2") as blk:
            for en in ENGS:
                getattr(blk, en)(runner(SC.ops[en], sems))

    if MK_PHASES < 3:
        st.close()
        return nc
    with ExitStack() as p3:
        SC.new_phase()
        barrier_in(2)
        e = p3.enter_context
        wo_bf = e(nc.sbuf_tensor("wo_bf", [128, 16, 2048], BF16))
        wst3 = e(nc.sbuf_tensor("wst3", [128, 2, 2048], F32))
        aoT_t = e(nc.sbuf_tensor("aoT_t", [128, 16, 512], BF16))
        o2 = e(nc.sbuf_tensor("o2", [128, 16, 512], F32))
        xto_t = e(nc.sbuf_tensor("xto_t", [128, 16, 512], F32))
        sqt = e(nc.sbuf_tensor("sqt", [128, 512], BF16))
        o2b = e(nc.sbuf_tensor("o2b", [128, 16, 512], BF16))
        ttmp = e(nc.sbuf_tensor("ttmp", [128, 512], F32))
        xnst = e(nc.sbuf_tensor("xnst", [128, 2, 512], BF16))
        rows3 = e(nc.sbuf_tensor("rows3", [1, 8, 512], F32))
        gidx_s = e(nc.sbuf_tensor("gidx_s", [128, 64], mybir.dt.uint32))
        pp = [e(nc.psum_tensor(f"pp{i}", [128, 512], F32)) for i in range(4)]
        aT_view = aT_all.rearrange("r (a t) -> (r a) t", t=512)
        SC.emit("sync", lambda g: g.dma_start(gidx_s[:], gidx), inc="sQ", inc_by=16)
        gix = SC.last_mark

        cm3 = []
        xto_mark = None
        for d in range(16):
            slot = d % 2
            wv = [cm3[-2]] if d >= 2 else ()
            SC.emit("sync", (lambda d=d, slot=slot: lambda g: g.dma_start(
                wst3[:, slot, :], wo[d * 128:(d + 1) * 128, :]))(),
                waits=wv, inc="sQ", inc_by=16)
            wom = SC.last_mark
            SC.emit("scalar", (lambda d=d, slot=slot: lambda g: g.activation(
                wo_bf[:, d, :], wst3[:, slot, :], AF.Copy))(),
                waits=(wom,), inc="sS")
            cm3.append(("sS", SC.val("sS")))
        wo_ready = SC.val("sS")

        for ch in range(4):
            T0 = 512 * ch
            ao_g = []
            for dt in range(16):
                SC.emit("gpsimd",
                        (lambda dt=dt, ch=ch: lambda g: g.indirect_dma_start(
                            aoT_t[:, dt, :], None, aT_view,
                            bass.IndirectOffsetOnAxis(gidx_s[:, ch * 16 + dt:ch * 16 + dt + 1], 0)))(),
                        waits=(("sC", 1), gix, ("sP", SC.val("sP")),
                               ("sV", SC.val("sV")), ("sS", SC.val("sS"))),
                        inc="sG", inc_by=16)
                ao_g.append(SC.last_mark)
            SC.emit("sync", (lambda T0=T0: lambda g: g.dma_start(
                xto_t[:], xTo[:, T0:T0 + 512].rearrange("(a p) t -> p a t", p=128)))(),
                waits=[("sV", SC.val("sV"))] + ([xto_mark] if xto_mark else []),
                inc="sQ", inc_by=16)
            xto_mark = SC.last_mark
            ao_ld = SC.last_mark
            for dm in range(16):
                bnk = dm % 2
                wv = [ao_ld, ("sS", wo_ready)] + ao_g
                if dm >= 2:
                    wv.append(("sS", SC.val("sS")))
                else:
                    wv.append(("sV", SC.val("sV")))
                for kt in range(16):
                    SC.emit("tensor", (lambda dm=dm, kt=kt, bnk=bnk: lambda g: g.matmul(
                        pp[bnk][:, :], wo_bf[:, kt, dm * 128:(dm + 1) * 128],
                        aoT_t[:, kt, :], start=(kt == 0), stop=(kt == 15)))(),
                        waits=wv if kt == 0 else (),
                        inc="sP" if kt == 15 else None)
                mm_done = SC.val("sP")
                SC.emit("scalar", (lambda dm=dm, bnk=bnk: lambda g: g.activation(
                    o2[:, dm, :], pp[bnk][:, :], AF.Copy))(),
                    waits=[("sP", mm_done)] + (SC.allq("sG") if dm < 2 else []),
                    inc="sS")
                SC.emit("scalar", (lambda dm=dm, bnk=bnk: lambda g: g.activation(
                    o2b[:, dm, :], pp[bnk][:, :], AF.Copy))())
                SC.emit("scalar", (lambda dm=dm: lambda g: g.activation(
                    sqt[:], o2[:, dm, :], AF.Square))(), inc="sS")
                sq_done = SC.val("sS")
                wv2 = [("sS", sq_done)]
                if dm == 0:
                    wv2.append(("sV", SC.val("sV")))
                SC.emit("tensor", (lambda dm=dm: lambda g: g.matmul(
                    pp[2][0:1, :], ones_b[:], o2b[:, dm, :],
                    start=(dm == 0), stop=(dm == 15)))(), waits=wv2)
                SC.emit("tensor", (lambda dm=dm: lambda g: g.matmul(
                    pp[3][0:1, :], ones_b[:], sqt[:],
                    start=(dm == 0), stop=(dm == 15)))(), inc="sP")
            st_done = SC.val("sP")
            SC.emit("vector", lambda g: g.tensor_scalar_mul(
                rows3[0:1, 0, :], pp[2][0:1, :], 1.0 / 2048.0),
                waits=(("sP", st_done),))
            SC.emit("vector", lambda g: g.tensor_scalar_mul(
                rows3[0:1, 1, :], pp[3][0:1, :], 1.0 / 2048.0), inc="sV")
            m2 = ln_rows(rows3, 0, 1, 2, 3, "sV", SC.val("sV"), ceps[0:1, 0:1], None)
            SC.emit("tensor", lambda g: g.matmul(
                pp[2][:, :], ones_r[:], rows3[0:1, 2, :], start=True, stop=True),
                waits=(("sV", m2),))
            SC.emit("tensor", lambda g: g.matmul(
                pp[3][:, :], ones_r[:], rows3[0:1, 3, :], start=True, stop=True),
                inc="sP")
            bc2 = SC.val("sP")
            x2_marks = []
            for dm in range(16):
                SC.emit("vector", (lambda dm=dm: lambda g: g.tensor_tensor(
                    ttmp[:], o2[:, dm, :], pp[2][:, :], ALU.mult))(),
                    waits=(("sP", bc2),))
                SC.emit("vector", (lambda dm=dm: lambda g: g.tensor_tensor(
                    ttmp[:], ttmp[:], pp[3][:, :], ALU.subtract))())
                SC.emit("vector", (lambda dm=dm: lambda g: g.tensor_add(
                    o2[:, dm, :], ttmp[:], xto_t[:, dm, :]))(), inc="sV")
                x2_marks.append(SC.val("sV"))
                SC.emit("gpsimd", (lambda dm=dm, T0=T0: lambda g: g.dma_start(
                    x2_sc[dm * 128:(dm + 1) * 128, T0:T0 + 512], o2[:, dm, :]))(),
                    waits=(("sV", x2_marks[-1]),), inc="sG", inc_by=16)
                SC.emit("scalar", (lambda dm=dm: lambda g: g.activation(
                    o2b[:, dm, :], o2[:, dm, :], AF.Copy))(),
                    waits=(("sV", x2_marks[-1]), ("sP", SC.val("sP"))))
                SC.emit("scalar", (lambda dm=dm: lambda g: g.activation(
                    sqt[:], o2[:, dm, :], AF.Square))(), inc="sS")
                wv3 = [("sS", SC.val("sS"))]
                SC.emit("tensor", (lambda dm=dm: lambda g: g.matmul(
                    pp[0][0:1, :], ones_b[:], o2b[:, dm, :],
                    start=(dm == 0), stop=(dm == 15)))(), waits=wv3)
                SC.emit("tensor", (lambda dm=dm: lambda g: g.matmul(
                    pp[1][0:1, :], ones_b[:], sqt[:],
                    start=(dm == 0), stop=(dm == 15)))(), inc="sP")
            st3 = SC.val("sP")
            SC.emit("vector", lambda g: g.tensor_scalar_mul(
                rows3[0:1, 4, :], pp[0][0:1, :], 1.0 / 2048.0),
                waits=(("sP", st3),))
            SC.emit("vector", lambda g: g.tensor_scalar_mul(
                rows3[0:1, 5, :], pp[1][0:1, :], 1.0 / 2048.0), inc="sV")
            m3 = ln_rows(rows3, 4, 5, 6, 7, "sV", SC.val("sV"), ceps[0:1, 0:1], None)
            SC.emit("tensor", lambda g: g.matmul(
                pp[0][:, :], ones_r[:], rows3[0:1, 6, :], start=True, stop=True),
                waits=(("sV", m3),))
            SC.emit("tensor", lambda g: g.matmul(
                pp[1][:, :], ones_r[:], rows3[0:1, 7, :], start=True, stop=True),
                inc="sP")
            bc3 = SC.val("sP")
            for dm in range(16):
                slot = dm % 2
                wv = [("sP", bc3)]
                wv.extend(SC.allq("sG"))
                SC.emit("vector", (lambda dm=dm: lambda g: g.tensor_tensor(
                    ttmp[:], o2[:, dm, :], pp[0][:, :], ALU.mult))(), waits=wv)
                SC.emit("vector", (lambda dm=dm, slot=slot: lambda g: g.tensor_tensor(
                    xnst[:, slot, :], ttmp[:], pp[1][:, :], ALU.subtract))(),
                    inc="sV")
                xn_done = SC.val("sV")
                SC.emit("gpsimd", (lambda dm=dm, slot=slot, T0=T0: lambda g: g.dma_start(
                    xn2_sc[dm * 128:(dm + 1) * 128, T0:T0 + 512], xnst[:, slot, :]))(),
                    waits=(("sV", xn_done),), inc="sG", inc_by=16)
        barrier_out()
        with nc.Block("ph3") as blk:
            for en in ENGS:
                getattr(blk, en)(runner(SC.ops[en], sems))

    if MK_PHASES < 4:
        st.close()
        return nc
    with ExitStack() as p4:
        SC.new_phase()
        barrier_in(3)
        e = p4.enter_context
        xn2h = e(nc.sbuf_tensor("xn2h", [128, 16, 1024], BF16))
        acc = e(nc.sbuf_tensor("acc", [128, 16, 1024], F32))
        wgi = e(nc.sbuf_tensor("wgi", [128, 16, 128], I32))
        wgb = e(nc.sbuf_tensor("wgb", [128, 4, 16, 128], BF16))
        hT = e(nc.sbuf_tensor("hT", [128, 4, 1024], BF16))
        wdi = e(nc.sbuf_tensor("wdi", [128, 2048], I32))
        wdb = e(nc.sbuf_tensor("wdb", [128, 4, 2048], BF16))
        ut = e(nc.sbuf_tensor("ut", [128, 1024], F32))
        rows4 = e(nc.sbuf_tensor("rows4", [1, 4, 1024], F32))
        csum = e(nc.sbuf_tensor("csum", [1, 2, 1024], BF16))
        m128b = e(nc.sbuf_tensor("m128b", [1, 128], BF16))
        dscb = e(nc.sbuf_tensor("dscb", [128, 16], BF16))
        dsc2b = e(nc.sbuf_tensor("dsc2b", [128, 16], BF16))
        sqb = e(nc.sbuf_tensor("sqb", [128, 1024], BF16))
        accb = e(nc.sbuf_tensor("accb", [128, 1024], BF16))
        x2t = e(nc.sbuf_tensor("x2t", [128, 2, 1024], F32))
        outst = e(nc.sbuf_tensor("outst", [128, 2, 1024], F32))
        sq4 = e(nc.sbuf_tensor("sq4", [128, 1024], F32))
        pg4 = [e(nc.psum_tensor(f"pg4{i}", [128, 512], F32)) for i in range(2)]
        pu4 = [e(nc.psum_tensor(f"pu4{i}", [128, 512], F32)) for i in range(2)]
        pd4 = [e(nc.psum_tensor(f"pd4{i}", [128, 512], F32)) for i in range(4)]
        out_mark = [None, None]

        SC.emit("vector", lambda g: g.memset(m128b[:], -128.0), inc="sV")
        SC.emit("scalar", lambda g: g.activation(dscb[:], dsc_s[:], AF.Copy),
                waits=(("sB", 15),))
        SC.emit("scalar", lambda g: g.activation(dsc2b[:], dsc2_s[:], AF.Copy),
                inc="sS")
        for half in range(2):
            H0 = 1024 * half
            SC.emit("sync", (lambda H0=H0: lambda g: g.dma_start(
                xn2h[:], xn2_sc[:, H0:H0 + 1024].rearrange("(a p) t -> p a t", p=128)))(),
                waits=tuple(SC.allq("sG")) + (("sV", SC.val("sV")),
                       ("sP", SC.val("sP")), ("sS", SC.val("sS"))),
                inc="sQ", inc_by=16)
            xl = SC.last_mark
            for d in range(16):
                SC.emit("tensor", (lambda d=d: lambda g: g.matmul(
                    pg4[0][0:1, :], ones_b[:], xn2h[:, d, 0:512],
                    start=(d == 0), stop=(d == 15)))(),
                    waits=[xl, ("sV", SC.val("sV"))] if d == 0 else ())
                SC.emit("tensor", (lambda d=d: lambda g: g.matmul(
                    pg4[1][0:1, :], ones_b[:], xn2h[:, d, 512:1024],
                    start=(d == 0), stop=(d == 15)))(), inc="sP")
            cs_done = SC.val("sP")
            SC.emit("scalar", lambda g: g.activation(
                csum[0:1, 0, 0:512], pg4[0][0:1, 0:512], AF.Copy),
                waits=(("sP", cs_done),))
            SC.emit("scalar", lambda g: g.activation(
                csum[0:1, 0, 512:1024], pg4[1][0:1, 0:512], AF.Copy), inc="sS")
            csx = SC.val("sS")

            for ic in range(16):
                hm = []
                for it4 in range(4):
                    itile = ic * 4 + it4
                    i0 = itile * 128
                    ip = itile % 2
                    bp = itile % 2
                    wv = [("sS", SC.val("sS"))] if itile >= 1 else [("sS", csx)]
                    SC.emit("sync", (lambda i0=i0: lambda g: g.dma_start(
                        wgi[:],
                        wgT[:, i0:i0 + 128].rearrange("(a p) t -> p a t", p=128)))(),
                        waits=wv, inc="sQ", inc_by=16)
                    gq = SC.last_mark
                    SC.emit("scalar", (lambda bp=bp: lambda g: g.activation(
                        wgb[:, bp, :, :], wgi[:], AF.Copy))(),
                        waits=(gq, ("sP", SC.val("sP"))), inc="sS")
                    gcast = SC.val("sS")
                    SC.emit("sync", (lambda i0=i0: lambda g: g.dma_start(
                        wgi[:],
                        wuT[:, i0:i0 + 128].rearrange("(a p) t -> p a t", p=128)))(),
                        waits=(("sS", gcast),), inc="sQ", inc_by=16)
                    uq = SC.last_mark
                    SC.emit("scalar", (lambda bp=bp: lambda g: g.activation(
                        wgb[:, 2 + bp, :, :], wgi[:], AF.Copy))(),
                        waits=(uq, ("sP", SC.val("sP"))), inc="sS")
                    ucast = SC.val("sS")
                    wv = [("sS", gcast), xl, ("sV", SC.val("sV"))]
                    for nh in range(2):
                        for d in range(16):
                            SC.emit("tensor",
                                    (lambda d=d, nh=nh, bp=bp: lambda g: g.matmul(
                                        pg4[nh][:, :], wgb[:, bp, d, :],
                                        xn2h[:, d, nh * 512:(nh + 1) * 512],
                                        start=(d == 0), stop=False))(),
                                    waits=wv if (nh == 0 and d == 0) else ())
                        SC.emit("tensor", (lambda nh=nh: lambda g: g.matmul(
                            pg4[nh][:, :], m128b[:],
                            csum[0:1, 0, nh * 512:(nh + 1) * 512],
                            start=False, stop=True))())
                    wv = [("sS", ucast)]
                    for nh in range(2):
                        for d in range(16):
                            SC.emit("tensor",
                                    (lambda d=d, nh=nh, bp=bp: lambda g: g.matmul(
                                        pu4[nh][:, :], wgb[:, 2 + bp, d, :],
                                        xn2h[:, d, nh * 512:(nh + 1) * 512],
                                        start=(d == 0), stop=False))(),
                                    waits=wv if (nh == 0 and d == 0) else ())
                        SC.emit("tensor", (lambda nh=nh: lambda g: g.matmul(
                            pu4[nh][:, :], m128b[:],
                            csum[0:1, 0, nh * 512:(nh + 1) * 512],
                            start=False, stop=True))(),
                            inc="sP" if nh == 1 else None)
                    gu_done = SC.val("sP")
                    SC.emit("scalar", (lambda itile=itile: lambda g: g.activation(
                        sq4[:, 0:512], pg4[0][:, :], AF.Silu,
                        scale=gsc_s[:, itile:itile + 1]))(),
                        waits=(("sP", gu_done), ("sV", SC.val("sV"))))
                    SC.emit("scalar", (lambda itile=itile: lambda g: g.activation(
                        sq4[:, 512:1024], pg4[1][:, :], AF.Silu,
                        scale=gsc_s[:, itile:itile + 1]))(), inc="sS")
                    silu_done = SC.val("sS")
                    SC.emit("vector", (lambda itile=itile: lambda g: g.tensor_scalar(
                        ut[:, 0:512], pu4[0][:, :], usc_s[:, itile:itile + 1],
                        None, ALU.mult))(),
                        waits=(("sP", gu_done),) + ((("sP", SC.val("sP")),) if it4 == 0 else ()))
                    SC.emit("vector", (lambda itile=itile: lambda g: g.tensor_scalar(
                        ut[:, 512:1024], pu4[1][:, :], usc_s[:, itile:itile + 1],
                        None, ALU.mult))())
                    SC.emit("vector", (lambda it4=it4: lambda g: g.tensor_tensor(
                        hT[:, it4, :], sq4[:, :], ut[:, :], ALU.mult))(),
                        waits=(("sS", silu_done),), inc="sV")
                    hm.append(SC.val("sV"))
                h_done = hm[-1]
                for it4 in range(4):
                    SC.emit("tensor", (lambda it4=it4: lambda g: g.matmul(
                        pg4[0][0:1, 0:512], ones_b[:], hT[:, it4, 0:512],
                        start=(it4 == 0), stop=(it4 == 3)))(),
                        waits=[("sV", h_done)] if it4 == 0 else ())
                    SC.emit("tensor", (lambda it4=it4: lambda g: g.matmul(
                        pg4[1][0:1, 0:512], ones_b[:], hT[:, it4, 512:1024],
                        start=(it4 == 0), stop=(it4 == 3)))(), inc="sP")
                csh_done = SC.val("sP")
                SC.emit("scalar", lambda g: g.activation(
                    csum[0:1, 1, 0:512], pg4[0][0:1, 0:512], AF.Copy),
                    waits=(("sP", csh_done),))
                SC.emit("scalar", lambda g: g.activation(
                    csum[0:1, 1, 512:1024], pg4[1][0:1, 0:512], AF.Copy), inc="sS")
                csh = SC.val("sS")
                for it4 in range(4):
                    i0 = (ic * 4 + it4) * 128
                    ip = it4 % 2
                    wv = [("sS", SC.val("sS"))] if it4 >= 1 else [("sS", csh)]
                    SC.emit("sync", (lambda i0=i0: lambda g: g.dma_start(
                        wdi[:], wdT[i0:i0 + 128, :]))(),
                        waits=wv, inc="sQ", inc_by=16)
                    wdm = SC.last_mark
                    SC.emit("scalar", (lambda it4=it4: lambda g: g.activation(
                        wdb[:, it4, :], wdi[:], AF.Copy))(),
                        waits=(wdm, ("sP", SC.val("sP"))), inc="sS")
                wd_ready = SC.val("sS")
                for dm in range(16):
                    for nh in range(2):
                        pd = pd4[(dm % 2) * 2 + nh]
                        wv = [("sS", wd_ready), ("sV", h_done)]
                        if ic > 0 or dm >= 2:
                            wv.append(("sV", SC.val("sV")))
                        for it4 in range(4):
                            SC.emit("tensor",
                                    (lambda dm=dm, nh=nh, it4=it4, pd=pd: lambda g: g.matmul(
                                        pd[:, :], wdb[:, it4, dm * 128:(dm + 1) * 128],
                                        hT[:, it4, nh * 512:(nh + 1) * 512],
                                        start=(it4 == 0), stop=False))(),
                                    waits=wv if it4 == 0 else ())
                        SC.emit("tensor", (lambda nh=nh, pd=pd: lambda g: g.matmul(
                            pd[:, :], m128b[:],
                            csum[0:1, 1, nh * 512:(nh + 1) * 512],
                            start=False, stop=True))(), inc="sP")
                        dmm = SC.val("sP")
                        if ic == 0:
                            SC.emit("vector",
                                    (lambda dm=dm, nh=nh, pd=pd: lambda g: g.tensor_copy(
                                        acc[:, dm, nh * 512:(nh + 1) * 512], pd[:, :]))(),
                                    waits=(("sP", dmm),), inc="sV")
                        else:
                            SC.emit("vector",
                                    (lambda dm=dm, nh=nh, pd=pd: lambda g: g.tensor_tensor(
                                        acc[:, dm, nh * 512:(nh + 1) * 512],
                                        acc[:, dm, nh * 512:(nh + 1) * 512],
                                        pd[:, :], ALU.add))(),
                                    waits=(("sP", dmm),), inc="sV")
            acc_done = SC.val("sV")
            for dm in range(16):
                SC.emit("scalar", (lambda dm=dm: lambda g: g.activation(
                    accb[:, :], acc[:, dm, :], AF.Copy))(),
                    waits=(("sV", acc_done), ("sP", SC.val("sP"))))
                SC.emit("scalar", (lambda dm=dm: lambda g: g.activation(
                    sqb[:, :], acc[:, dm, :], AF.Square))(), inc="sS")
                sqd = SC.val("sS")
                for nh in range(2):
                    wv = [("sS", sqd)]
                    SC.emit("tensor", (lambda dm=dm, nh=nh: lambda g: g.matmul(
                        pg4[nh][0:1, 0:512], dscb[:, dm:dm + 1],
                        accb[:, nh * 512:(nh + 1) * 512],
                        start=(dm == 0), stop=(dm == 15)))(),
                        waits=wv if nh == 0 else ())
                    SC.emit("tensor", (lambda dm=dm, nh=nh: lambda g: g.matmul(
                        pu4[nh][0:1, 0:512], dsc2b[:, dm:dm + 1],
                        sqb[:, nh * 512:(nh + 1) * 512],
                        start=(dm == 0), stop=(dm == 15)))(), inc="sP")
            st4 = SC.val("sP")
            for nh in range(2):
                SC.emit("vector", (lambda nh=nh: lambda g: g.tensor_scalar_mul(
                    rows4[0:1, 0, nh * 512:(nh + 1) * 512],
                    pg4[nh][0:1, 0:512], 1.0 / 2048.0))(),
                    waits=(("sP", st4),))
                SC.emit("vector", (lambda nh=nh: lambda g: g.tensor_scalar_mul(
                    rows4[0:1, 1, nh * 512:(nh + 1) * 512],
                    pu4[nh][0:1, 0:512], 1.0 / 2048.0))(), inc="sV")
            m4 = ln_rows(rows4, 0, 1, 2, 3, "sV", SC.val("sV"), ceps[0:1, 0:1], None)
            for nh in range(2):
                SC.emit("tensor", (lambda nh=nh: lambda g: g.matmul(
                    pg4[nh][:, :], ones_r[:],
                    rows4[0:1, 2, nh * 512:(nh + 1) * 512], start=True, stop=True))(),
                    waits=(("sV", m4),))
                SC.emit("tensor", (lambda nh=nh: lambda g: g.matmul(
                    pu4[nh][:, :], ones_r[:],
                    rows4[0:1, 3, nh * 512:(nh + 1) * 512], start=True, stop=True))(),
                    inc="sP")
            bc4 = SC.val("sP")
            for dm in range(16):
                slot = dm % 2
                wv_dma = [("sV", SC.val("sV"))]
                if dm >= 2:
                    wv_dma.extend(SC.allq("sG"))
                SC.emit("sync", (lambda dm=dm, slot=slot, H0=H0: lambda g: g.dma_start(
                    x2t[:, slot, :], x2_sc[dm * 128:(dm + 1) * 128, H0:H0 + 1024]))(),
                    waits=wv_dma, inc="sQ", inc_by=16)
                x2l = SC.last_mark
                SC.emit("vector", (lambda dm=dm: lambda g: g.tensor_scalar(
                    sq4[:, :], acc[:, dm, :], dsc_s[:, dm:dm + 1], None, ALU.mult))(),
                    waits=[("sP", bc4), ("sS", SC.val("sS"))] +
                          ([out_mark[slot]] if out_mark[slot] else []))
                for nh in range(2):
                    SC.emit("vector", (lambda dm=dm, nh=nh: lambda g: g.tensor_tensor(
                        sq4[:, nh * 512:(nh + 1) * 512],
                        sq4[:, nh * 512:(nh + 1) * 512], pg4[nh][:, :], ALU.mult))())
                    SC.emit("vector", (lambda dm=dm, nh=nh, slot=slot: lambda g: g.tensor_tensor(
                        outst[:, slot, nh * 512:(nh + 1) * 512],
                        sq4[:, nh * 512:(nh + 1) * 512], pu4[nh][:, :], ALU.subtract))())
                SC.emit("vector", (lambda dm=dm, slot=slot: lambda g: g.tensor_add(
                    outst[:, slot, :], outst[:, slot, :], x2t[:, slot, :]))(),
                    waits=(x2l,), inc="sV")
                out_done = SC.val("sV")
                SC.emit("gpsimd", (lambda dm=dm, slot=slot, H0=H0: lambda g: g.dma_start(
                    outT[dm * 128:(dm + 1) * 128, H0:H0 + 1024], outst[:, slot, :]))(),
                    waits=(("sV", out_done),), inc="sG", inc_by=16)
                out_mark[slot] = SC.last_mark
        SC.emit("gpsimd", lambda g: g.nop(), waits=tuple(SC.allq("sG")))
        with nc.Block("ph4") as blk:
            for en in ENGS:
                getattr(blk, en)(runner(SC.ops[en], sems))

    st.close()
    return nc


def kernel(L0_attn_x, L0_attn_Wq, L0_attn_Wkv_a, L0_attn_kv_norm_w, L0_attn_Wkv_b,
           L0_attn_Wo, L0_ffn_gate_w, L0_ffn_gate_scale, L0_ffn_up_w,
           L0_ffn_up_scale, L0_ffn_down_w, L0_ffn_down_scale):
    x = np.asarray(L0_attn_x)
    Wq = np.asarray(L0_attn_Wq)
    Wkv_a = np.asarray(L0_attn_Wkv_a)
    gw = np.asarray(L0_attn_kv_norm_w)
    Wkv_b = np.asarray(L0_attn_Wkv_b)
    Wo = np.asarray(L0_attn_Wo)
    gate_w = np.asarray(L0_ffn_gate_w)
    up_w = np.asarray(L0_ffn_up_w)
    down_w = np.asarray(L0_ffn_down_w)

    xT = [np.ascontiguousarray(x[b].T) for b in range(2)]
    kr = Wkv_a[:, 512:576]
    Akr = np.concatenate([kr[:, 0::2], kr[:, 1::2]], axis=1)
    Bkr = np.concatenate([kr[:, 1::2], kr[:, 0::2]], axis=1)
    wkva = np.ascontiguousarray(
        np.concatenate([Wkv_a[:, :512], Akr, Bkr, Bkr, Akr], axis=1))
    wq_p, wkvb_p = [], []
    for p in range(4):
        H = range(4 * p, 4 * p + 4)
        nope = np.concatenate([Wq[:, h * 192:h * 192 + 128] for h in H], axis=1)
        A = np.concatenate(
            [np.concatenate([Wq[:, h * 192 + 128:h * 192 + 192][:, 0::2],
                             Wq[:, h * 192 + 128:h * 192 + 192][:, 1::2]], axis=1)
             for h in H], axis=1)
        Bm = np.concatenate(
            [np.concatenate([Wq[:, h * 192 + 128:h * 192 + 192][:, 1::2],
                             Wq[:, h * 192 + 128:h * 192 + 192][:, 0::2]], axis=1)
             for h in H], axis=1)
        wq_p.append(np.ascontiguousarray(np.concatenate([nope, A, Bm], axis=1)))
        kn = np.concatenate([Wkv_b[:, h * 256:h * 256 + 128] for h in H], axis=1)
        vv = np.concatenate([Wkv_b[:, h * 256 + 128:h * 256 + 256] for h in H], axis=1)
        wkvb_p.append(np.ascontiguousarray(np.concatenate([kn, vv], axis=1)))
    kvgain = np.ascontiguousarray(np.asarray(gw, np.float32).reshape(4, 128).T)
    wgT = np.ascontiguousarray(gate_w.T)
    wuT = np.ascontiguousarray(up_w.T)
    wdT = np.ascontiguousarray(down_w.T)
    gsc = np.ascontiguousarray(np.asarray(L0_ffn_gate_scale).reshape(64, 128).T)
    usc = np.ascontiguousarray(np.asarray(L0_ffn_up_scale).reshape(64, 128).T)
    dsc = np.ascontiguousarray(np.asarray(L0_ffn_down_scale).reshape(16, 128).T)

    freqs = 10000.0 ** (-np.arange(0, 64, 2, dtype=np.float64) / 64.0)
    ang = np.outer(freqs, np.arange(S, dtype=np.float64))
    c32 = np.cos(ang).astype(np.float32)
    s32 = np.sin(ang).astype(np.float32)
    ropeC = np.ascontiguousarray(np.tile(c32, (4, 1)))
    ropeS = np.ascontiguousarray(np.concatenate([-s32, s32, -s32, s32], axis=0))
    tri = np.triu(np.ones((128, 128), dtype=np.float32)).astype(BF)
    ident = np.eye(128, dtype=np.float32).astype(BF)

    nc = build_nc()
    in_maps = []
    for c in range(8):
        b, p = c // 4, c % 4
        si = np.empty((128, 4), dtype=np.uint32)
        for hs4 in range(4):
            si[:, hs4] = 512 * p + hs4 * 128 + np.arange(128)
        gi = np.empty((128, 64), dtype=np.uint32)
        for ch in range(4):
            for dt in range(16):
                gi[:, ch * 16 + dt] = (dt * 128 + np.arange(128)) * 16 + p * 4 + ch
        in_maps.append({
            "xT": xT[b], "gidx": gi, "sidx": si,
            "xTo": np.ascontiguousarray(xT[b][:, p * 2048:(p + 1) * 2048]),
            "wq": wq_p[p], "wkva": wkva, "wkvb": wkvb_p[p], "kvgain": kvgain,
            "wo": Wo, "wgT": wgT, "wuT": wuT, "wdT": wdT,
            "gsc": gsc, "usc": usc, "dsc": dsc,
            "ropeC": ropeC, "ropeS": ropeS, "tri": tri, "ident": ident,
        })
    res = run_bass_kernel_spmd(nc, in_maps, core_ids=list(range(8)))
    out = np.empty((2, S, D), dtype=np.float32)
    for c in range(8):
        b, p = c // 4, c % 4
        out[b, p * 2048:(p + 1) * 2048, :] = res.results[c]["outT"].T
    return out
